# version 02
# baseline (speedup 1.0000x reference)
import contextlib
from collections import defaultdict

import numpy as np
import ml_dtypes

import concourse.bass as bass
import concourse.mybir as mybir
from concourse.bass_utils import run_bass_kernel_spmd

F32 = mybir.dt.float32
BF16 = mybir.dt.bfloat16
AF = mybir.ActivationFunctionType
ALU = mybir.AluOpType
AX = mybir.AxisListType

NCORES = 8
NTOK = 2048
NCH = 16
CH = 128
NH = 16
HD = 64
D = 1024
DU = 8
FF = 4096
NE = 8
BSLAB = 1024
NSLAB = FF // BSLAB
NFS = BSLAB // 128
EPS = 1e-5
SINK_ITERS = 20

ENGINES = ("sync", "gpsimd", "tensor", "vector", "scalar")


def _gammas64():
    lin = np.linspace(np.log(1.0 / 512.0), np.log(1.0 / 32.0), NH)
    return 1.0 - np.exp(lin)


class Plan:
    def __init__(self):
        self.ops = {e: [] for e in ENGINES}
        self.cnt = defaultdict(int)
        self.lastw = {}

    def wait(self, eng, sem, val):
        if val is None or val <= 0:
            return
        assert val <= self.cnt[sem], (
            f"forward ref: {eng} waits {sem}>={val}, emitted {self.cnt[sem]}")
        if self.lastw.get((eng, sem), 0) >= val:
            return
        self.lastw[(eng, sem)] = val
        self.ops[eng].append(("w", sem, val))

    def op(self, eng, fn, inc=None, n=1):
        self.ops[eng].append(("o", fn, inc, n))
        if inc is not None:
            self.cnt[inc] += n
            return self.cnt[inc]
        return None

    def dma(self, eng, fn, sem):
        return self.op(eng, fn, sem, 16)


def build(taps=()):
    nc = bass.Bass("TRN2", target_bir_lowering=False, debug=False)
    g64 = _gammas64()

    x0_d = nc.dram_tensor("x0", [NTOK, D], F32, kind="ExternalInput").ap()
    wqkv_d = nc.dram_tensor("wqkv", [128, 3 * NH * HD], F32, kind="ExternalInput").ap()
    w1_d = nc.dram_tensor("w1", [D, D], F32, kind="ExternalInput").ap()
    w2_d = nc.dram_tensor("w2", [D, D], F32, kind="ExternalInput").ap()
    gnsc_d = nc.dram_tensor("gnscb", [128, D], F32, kind="ExternalInput").ap()
    gnbi_d = nc.dram_tensor("gnbib", [128, D], F32, kind="ExternalInput").ap()
    gw_d = nc.dram_tensor("gwT", [D, NE], F32, kind="ExternalInput").ap()
    wg_d = nc.dram_tensor("WgT", [NE, D, FF], F32, kind="ExternalInput").ap()
    wu_d = nc.dram_tensor("WuT", [NE, D, FF], F32, kind="ExternalInput").ap()
    wd_d = nc.dram_tensor("WdT", [NE, FF, D], F32, kind="ExternalInput").ap()
    ident_d = nc.dram_tensor("ident", [128, 128], BF16, kind="ExternalInput").ap()
    maskT_d = nc.dram_tensor("maskT", [NH, CH, CH], F32, kind="ExternalInput").ap()
    qgc_d = nc.dram_tensor("qgc", [CH, NH], F32, kind="ExternalInput").ap()
    kg_d = nc.dram_tensor("kg", [CH, NH], F32, kind="ExternalInput").ap()
    wsb_d = nc.dram_tensor("wsb", [HD, NH * NCORES], F32, kind="ExternalInput").ap()

    out_d = nc.dram_tensor("out", [NTOK, D], F32, kind="ExternalOutput").ap()

    cc1_in = nc.dram_tensor("cc1_in", [NH, HD, HD], F32)
    cc1_out = nc.dram_tensor("cc1_out", [NCORES, NH, HD, HD], F32, addr_space="Shared")
    cc2_in = nc.dram_tensor("cc2_in", [NTOK, NE], F32)
    cc2_out = nc.dram_tensor("cc2_out", [NCORES * NTOK, NE], F32, addr_space="Shared")
    ysp_kind = "ExternalOutput" if "y" in taps else "Internal"
    yspill = nc.dram_tensor("tap_y" if "y" in taps else "yspill",
                            [NTOK, D], F32, kind=ysp_kind)

    tap_d = {}
    def tapt(name, shape, dtype):
        tap_d[name] = nc.dram_tensor("tap_" + name, shape, dtype,
                                     kind="ExternalOutput").ap()
    if "tfm" in taps: tapt("tfm", [128, 16384], BF16)
    if "norm" in taps: tapt("norm", [128, 16384], BF16)
    if "attn2" in taps: tapt("attn2", [128, 16384], BF16)
    if "xffm" in taps: tapt("xffm", [128, 16384], BF16)
    if "logits" in taps: tapt("logits", [128, 128], F32)
    if "pi" in taps: tapt("pi", [128, 1024], F32)
    if "lpi" in taps: tapt("lpi", [128, 128], F32)
    if "wsel" in taps: tapt("wsel", [128, 128], F32)
    if "fpre" in taps: tapt("fpre", [128, 16384], F32)

    stack1 = contextlib.ExitStack()
    stack2 = contextlib.ExitStack()
    stack3 = contextlib.ExitStack()
    stack4 = contextlib.ExitStack()
    S1 = lambda *a, **k: stack1.enter_context(nc.sbuf_tensor(*a, **k))[:]
    S2 = lambda *a, **k: stack2.enter_context(nc.sbuf_tensor(*a, **k))[:]
    S3 = lambda *a, **k: stack3.enter_context(nc.sbuf_tensor(*a, **k))[:]
    S4 = lambda *a, **k: stack4.enter_context(nc.sbuf_tensor(*a, **k))[:]

    A = S1("Abig", [128, NCH * D], F32)
    B = S1("Bfm", [128, NCH * D], BF16)
    C = S1("Ctok", [128, NCH * D], BF16)
    wqkv = S1("wqkvb", [128, 3 * NH * HD], BF16)
    ident = S1("identb", [128, 128], BF16)
    kg = S1("kgb", [CH, NH], F32)
    gwb = S1("gwb", [128, DU * NE], BF16)
    onescol = S1("onescol", [128, 1], F32)
    epsb = S1("epsb", [128, 1], F32)
    onesrow = S1("onesrow", [1, 128], F32)
    spc = S1("spc", [128, 1], F32)

    bank = [stack1.enter_context(nc.psum_tensor(f"bk{i}", [128, 512], F32))[:]
            for i in range(6)]
    pbt = [stack1.enter_context(nc.psum_tensor(f"pbt{i}", [128, 1024], BF16))[:]
           for i in range(2)]

    sems = {}
    for s in ("sq", "gq", "cc", "tn", "vc", "sc", "doo"):
        sems[s] = stack1.enter_context(nc.semaphore(s))

    pl = Plan()

    def spacer():
        pl.op("vector", lambda e: e.tensor_copy(spc, onescol))

    Av = A.rearrange("p (i d) -> p i d", d=D)
    Bv = B.rearrange("p (u t) -> p u t", t=NTOK)
    Cv = C.rearrange("p (i d) -> p i d", d=D)

    x0v = x0_d.rearrange("(i p) d -> p i d", p=128)
    c_x0 = pl.dma("sync", lambda e: e.dma_start(out=Av, in_=x0v), "sq")
    c_ident = pl.dma("sync", lambda e: e.dma_start(out=ident, in_=ident_d[:, :]), "sq")
    c_kg = pl.dma("sync", lambda e: e.dma_start(out=kg, in_=kg_d[:, :]), "sq")

    c_wqkv = pl.dma("gpsimd", lambda e: e.dma_start(out=wqkv, in_=wqkv_d[:, :]), "gq")
    gwv = gw_d.rearrange("(c p) e -> p c e", p=128)
    c_gw = pl.dma("gpsimd", lambda e: e.dma_start(
        out=gwb.rearrange("p (c e) -> p c e", e=NE), in_=gwv), "gq")

    pl.op("vector", lambda e: e.memset(onescol, 1.0))
    pl.op("vector", lambda e: e.memset(epsb, EPS))
    c_ones = pl.op("vector", lambda e: e.memset(onesrow, 1.0), inc="vc")

    masks = [S2(f"masks{j}", [128, CH], F32) for j in range(2)]
    qgc = S2("qgcS", [CH, NH], F32)
    wsb = S2("wsbS", [HD, NH * NCORES], F32)
    gnsc = S2("gnscS", [128, D], BF16)
    gnbi = S2("gnbiS", [128, D], BF16)
    Lall = S2("Lall", [HD, NH * HD], F32)
    Lgc = S2("Lgc", [HD, NH * HD], F32)
    Sin = S2("Sin", [HD, NH * HD], F32)
    uni = S2("uni", [128, 2 * NCH * HD], BF16)
    sinb = uni[0:64, 0:NCH * HD]
    sbfb = uni[0:64, NCH * HD:2 * NCH * HD]
    Sbuf = [S2(f"Sb{j}", [HD, HD], F32) for j in range(2)]
    Qp = S2("Qp", [HD, NTOK], BF16)
    Kp = S2("Kp", [HD, NTOK], BF16)
    Vt = S2("Vt", [128, NCH * HD], BF16)
    Kst = [S2(f"Kst{j}", [128, HD], BF16) for j in range(2)]
    retb = S2("ret", [128, NCH * HD], F32)
    rtmp = [S2(f"rtmp{j}", [128, HD], F32) for j in range(2)]
    rsq = S2("rsq", [128, 4 * NCH], F32)
    gnstat = S2("gnst", [128, 2 * NCH], F32)
    sqd = [S2(f"sqd{j}", [128, HD], BF16) for j in range(2)]
    st6 = S2("st6", [128, 24], F32)
    mvs = S2("mvs", [128, 16], F32)
    sdv = S2("sdv", [128, 8], F32)
    tstage = [uni[:, 0:D], uni[:, D:2 * D]]
    smb = [S2(f"smb{j}", [128, CH], BF16) for j in range(2)]
    sg = [S2(f"sg{j}", [128, D], BF16) for j in range(2)]
    tmpa = S2("tmpa", [128, D], BF16)
    tmpb = S2("tmpb", [128, D], BF16)
    ytmp = S2("ytmp", [128, D], F32)
    ybuf = [S2("yb0", [128, D], F32)] * 2
    wb12 = S2("wb12", [128, DU * D], BF16)

    c_wsb = pl.dma("sync", lambda e: e.dma_start(out=wsb, in_=wsb_d[:, :]), "sq")
    c_qgc = pl.dma("sync", lambda e: e.dma_start(out=qgc, in_=qgc_d[:, :]), "sq")
    c_gnsc = pl.dma("gpsimd", lambda e: e.dma_start(out=gnsc, in_=gnsc_d[:, :]), "gq")
    c_gnbi = pl.dma("gpsimd", lambda e: e.dma_start(out=gnbi, in_=gnbi_d[:, :]), "gq")

    cnt_norm = [None] * NCH
    cnt_tr = [None] * NCH
    cnt_ev = [None] * NCH
    cnt_aggr = [None] * NCH
    cnt_std = [None] * NCH

    pl.wait("vector", "sq", c_x0)
    for i in range(NCH + 4):
        if i < NCH:
            s2 = (i % 2) * 12
            pl.op("vector", lambda e, i=i, s2=s2: e.bn_stats(
                st6[:, s2:s2 + 6], Av[:, i, 0:512]))
            pl.op("vector", lambda e, i=i, s2=s2: e.bn_stats(
                st6[:, s2 + 6:s2 + 12], Av[:, i, 512:1024]))
        j = i - 1
        if 0 <= j < NCH:
            s2 = (j % 2) * 12
            m4 = (j % 4) * 2
            cnt_aggr[j] = pl.op("vector", lambda e, s2=s2, m4=m4: e.bn_aggr(
                mvs[:, m4:m4 + 2], st6[:, s2:s2 + 12]), inc="vc")
            pl.wait("scalar", "vc", cnt_aggr[j])
            cnt_std[j] = pl.op("scalar", lambda e, m4=m4, j=j: e.activation(
                sdv[:, (j % 4) * 2:(j % 4) * 2 + 1], mvs[:, m4 + 1:m4 + 2],
                AF.Sqrt, bias=epsb), inc="sc")
        j = i - 2
        if 0 <= j < NCH:
            pl.wait("vector", "sc", cnt_std[j])
            pl.op("vector", lambda e, j=j: e.reciprocal(
                sdv[:, (j % 4) * 2 + 1:(j % 4) * 2 + 2],
                sdv[:, (j % 4) * 2:(j % 4) * 2 + 1]))
        j = i - 3
        if 0 <= j < NCH:
            if j >= 2 and cnt_tr[j - 2] is not None:
                pl.wait("vector", "tn", cnt_tr[j - 2])
            m4 = (j % 4) * 2
            cnt_norm[j] = pl.op("vector", lambda e, j=j, m4=m4: e.tensor_scalar(
                tstage[j % 2], Av[:, j, :], mvs[:, m4:m4 + 1],
                sdv[:, (j % 4) * 2 + 1:(j % 4) * 2 + 2],
                op0=ALU.subtract, op1=ALU.mult), inc="vc")
            pl.wait("tensor", "vc", cnt_norm[j])
            pl.wait("tensor", "sq", c_ident)
            if j >= 2 and cnt_ev[j - 2] is not None:
                pl.wait("tensor", "vc", cnt_ev[j - 2])
            for u in range(DU):
                cnt_tr[j] = pl.op("tensor", lambda e, j=j, u=u: e.transpose(
                    pbt[j % 2][:, 128 * u:128 * u + 128],
                    tstage[j % 2][:, 128 * u:128 * u + 128], ident),
                    inc="tn" if u == DU - 1 else None)
        j = i - 4
        if 0 <= j < NCH:
            pl.wait("vector", "tn", cnt_tr[j])
            cnt_ev[j] = pl.op("vector", lambda e, j=j: e.tensor_copy(
                Bv[:, :, 128 * j:128 * j + 128],
                pbt[j % 2].rearrange("p (u t) -> p u t", t=128)), inc="vc")
    c_tfm = pl.cnt["vc"]

    if "tfm" in taps:
        pl.wait("sync", "vc", c_tfm)
        pl.dma("sync", lambda e: e.dma_start(out=tap_d["tfm"][:, :], in_=B), "doo")

    def tfm_head(n, c0, w):
        base = 64 * (n % 2)
        u = n // 2
        return Bv[base:base + 64, u, c0:c0 + w]

    def wqkv_sl(n, m):
        base = 64 * (n % 2)
        return wqkv[base:base + 64, 1024 * m + 64 * n:1024 * m + 64 * n + 64]

    G128 = [float(np.float32(g64[n] ** 128)) for n in range(NH)]
    GI = [[float(np.float32(g64[n] ** (128 * i))) for i in range(NCH)]
          for n in range(NH)]

    pl.wait("tensor", "vc", c_tfm)
    pl.wait("tensor", "gq", c_wqkv)
    pl.wait("vector", "sq", c_kg)

    Vt_v = Vt.rearrange("p (i e) -> p i e", e=HD)
    sinb_v = sinb.rearrange("E (i e) -> E i e", e=HD)
    sbf_v = sbfb.rearrange("E (i e) -> E i e", e=HD)
    retv = retb.rearrange("p (i e) -> p i e", e=HD)

    cnt_kst_ev = {}
    cnt_vt_ev = {}
    cnt_stt = {}
    c_mmkv = {}
    for n in range(NH):
        for i in range(NCH):
            gi = n * NCH + i
            pk = bank[i % 2][:, 0:64]
            pv = bank[2 + i % 2][:, 0:64]
            pkv = bank[4 + i % 2][0:64, 0:64]
            if gi >= 2:
                n2, i2 = divmod(gi - 2, NCH)
                pl.wait("tensor", "vc", cnt_kst_ev.get((n2, i2)))
                pl.wait("tensor", "vc", cnt_vt_ev.get((n2, i2)))
                pl.wait("tensor", "vc", cnt_stt.get((n2, i2)))
            c_mk = pl.op("tensor", lambda e, n=n, i=i, pk=pk: e.matmul(
                pk, tfm_head(n, 128 * i, 128), wqkv_sl(n, 1),
                start=True, stop=True), inc="tn")
            c_mv = pl.op("tensor", lambda e, n=n, i=i, pv=pv: e.matmul(
                pv, tfm_head(n, 128 * i, 128), wqkv_sl(n, 2),
                start=True, stop=True), inc="tn")
            pl.wait("vector", "tn", c_mv)
            if gi >= 2:
                n2, i2 = divmod(gi - 2, NCH)
                pl.wait("vector", "tn", c_mmkv.get((n2, i2)))
            cnt_kst_ev[(n, i)] = pl.op("vector", lambda e, n=n, i=i, pk=pk: e.tensor_scalar(
                Kst[i % 2], pk, kg[:, n:n + 1], None, op0=ALU.mult), inc="vc")
            cnt_vt_ev[(n, i)] = pl.op("vector", lambda e, i=i, pv=pv: e.tensor_copy(
                Vt_v[:, i, :], pv), inc="vc")
            pl.wait("tensor", "vc", cnt_vt_ev[(n, i)])
            c_mmkv[(n, i)] = pl.op("tensor", lambda e, i=i, pkv=pkv: e.matmul(
                pkv, Kst[i % 2], Vt_v[:, i, :], start=True, stop=True), inc="tn")
            pl.wait("vector", "tn", c_mmkv[(n, i)])
            if i == 0:
                cnt_stt[(n, i)] = pl.op("vector", lambda e, pkv=pkv: e.tensor_copy(
                    Sbuf[0], pkv), inc="vc")
            else:
                cnt_stt[(n, i)] = pl.op("vector", lambda e, n=n, i=i, pkv=pkv: e.scalar_tensor_tensor(
                    Sbuf[i % 2], Sbuf[(i - 1) % 2], G128[n], pkv,
                    op0=ALU.mult, op1=ALU.add), inc="vc")
        pl.wait("vector", "vc", cnt_stt[(n, NCH - 1)])
        pl.op("vector", lambda e, n=n: e.tensor_copy(
            Lall[:, 64 * n:64 * n + 64], Sbuf[(NCH - 1) % 2]), inc="vc")
    c_Lall = pl.cnt["vc"]
    c_b1_tn = pl.cnt["tn"]

    pl.wait("gpsimd", "vc", c_Lall)
    c_l_out = pl.dma("gpsimd", lambda e: e.dma_start(
        out=cc1_in.ap().rearrange("n E e -> E n e"),
        in_=Lall.rearrange("E (n e) -> E n e", e=HD)), "gq")
    pl.wait("gpsimd", "gq", c_l_out)
    c_ag1 = pl.op("gpsimd", lambda e: e.collective_compute(
        "AllGather", ALU.bypass, replica_groups=[list(range(NCORES))],
        ins=[cc1_in.ap().opt()], outs=[cc1_out.ap().opt()]), inc="cc")
    pl.wait("gpsimd", "cc", c_ag1)
    pl.wait("vector", "sq", c_wsb)
    c_lgc = [None] * NCORES
    c_lgc_used = [None] * NCORES
    for cc_ in range(NCORES):
        if cc_ >= 1:
            pl.wait("gpsimd", "vc", c_lgc_used[cc_ - 1])
        c_lgc[cc_] = pl.dma("gpsimd", lambda e, cc_=cc_: e.dma_start(
            out=Lgc.rearrange("E (n e) -> E n e", e=HD),
            in_=cc1_out.ap()[cc_].rearrange("n E e -> E n e")), "gq")
        pl.wait("vector", "gq", c_lgc[cc_])
        src = Lgc.rearrange("E (n e) -> E n e", e=HD)
        for n in range(NH):
            dst = Sin[:, 64 * n:64 * n + 64]
            if cc_ == 0:
                cn = pl.op("vector", lambda e, n=n, src=src, dst=dst: e.tensor_scalar(
                    dst, src[:, n, :], wsb[:, 8 * n + 0:8 * n + 1],
                    None, op0=ALU.mult), inc="vc")
            else:
                cn = pl.op("vector", lambda e, n=n, cc_=cc_, src=src, dst=dst: e.scalar_tensor_tensor(
                    dst, src[:, n, :], wsb[:, 8 * n + cc_:8 * n + cc_ + 1],
                    dst, op0=ALU.mult, op1=ALU.add), inc="vc")
        c_lgc_used[cc_] = cn
    c_sin = [c_lgc_used[NCORES - 1]] * NH

    cnt_head_tn = [None] * NH
    cnt_head_sc = [None] * NH
    cnt_head_vc = [None] * NH
    cnt_head_sm = [None] * NH

    for n in range(NH):
        hp = n % 2
        if n >= 2:
            pl.wait("sync", "vc", cnt_head_sm[n - 2])
        c_mask_n = pl.dma("sync", lambda e, n=n: e.dma_start(
            out=masks[n % 2], in_=maskT_d[n]), "sq")
        if n == 0:
            pl.wait("tensor", "vc", c_Lall)
            pl.wait("vector", "tn", c_b1_tn)
        else:
            pl.wait("tensor", "vc", cnt_head_vc[n - 1])
            pl.wait("tensor", "sc", cnt_head_sc[n - 1])
            pl.wait("vector", "tn", cnt_head_tn[n - 1])
        c_qev = [None] * 4
        c_kev = [None] * 4
        for q in range(4):
            pq = bank[q % 2][0:64, 0:512]
            pk2 = bank[2 + q % 2][0:64, 0:512]
            if q >= 2:
                pl.wait("tensor", "vc", c_qev[q - 2])
                pl.wait("tensor", "vc", c_kev[q - 2])
            c_mmq = pl.op("tensor", lambda e, n=n, q=q, pq=pq: e.matmul(
                pq, wqkv_sl(n, 0), tfm_head(n, 512 * q, 512),
                start=True, stop=True), inc="tn")
            pl.wait("vector", "tn", c_mmq)
            c_qev[q] = pl.op("vector", lambda e, q=q, pq=pq: e.tensor_copy(
                Qp[:, 512 * q:512 * q + 512], pq), inc="vc")
            c_mmk2 = pl.op("tensor", lambda e, n=n, q=q, pk2=pk2: e.matmul(
                pk2, wqkv_sl(n, 1), tfm_head(n, 512 * q, 512),
                start=True, stop=True), inc="tn")
            pl.wait("vector", "tn", c_mmk2)
            c_kev[q] = pl.op("vector", lambda e, q=q, pk2=pk2: e.tensor_copy(
                Kp[:, 512 * q:512 * q + 512], pk2), inc="vc")
        c_vt = [None] * NCH
        c_kst = [None] * NCH
        c_chain = [None] * NCH
        c_kvmm = [None] * NCH
        for i in range(NCH):
            pvt = bank[i % 2][:, 0:64]
            pkt = bank[2 + i % 2][:, 0:64]
            pkv = bank[4 + i % 2][0:64, 0:64]
            if i >= 2:
                pl.wait("tensor", "vc", c_vt[i - 2])
                pl.wait("tensor", "vc", c_kst[i - 2])
                pl.wait("tensor", "vc", c_chain[i - 2])
            else:
                pl.wait("tensor", "vc", c_qev[3])
                pl.wait("tensor", "vc", c_kev[3])
            c_mv = pl.op("tensor", lambda e, n=n, i=i, pvt=pvt: e.matmul(
                pvt, tfm_head(n, 128 * i, 128), wqkv_sl(n, 2),
                start=True, stop=True), inc="tn")
            c_mk = pl.op("tensor", lambda e, n=n, i=i, pkt=pkt: e.matmul(
                pkt, tfm_head(n, 128 * i, 128), wqkv_sl(n, 1),
                start=True, stop=True), inc="tn")
            pl.wait("vector", "tn", c_mk)
            if i >= 2:
                pl.wait("vector", "tn", c_kvmm[i - 2])
            c_vt[i] = pl.op("vector", lambda e, i=i, hp=hp, pvt=pvt: e.tensor_copy(
                Vt_v[:, i, :], pvt), inc="vc")
            c_kst[i] = pl.op("vector", lambda e, n=n, i=i, pkt=pkt: e.tensor_scalar(
                Kst[i % 2], pkt, kg[:, n:n + 1], None, op0=ALU.mult), inc="vc")
            pl.wait("tensor", "vc", c_kst[i])
            c_kvmm[i] = pl.op("tensor", lambda e, i=i, hp=hp, pkv=pkv: e.matmul(
                pkv, Kst[i % 2], Vt_v[:, i, :], start=True, stop=True), inc="tn")
            pl.wait("vector", "tn", c_kvmm[i])
            if i == 0:
                c_chain[i] = pl.op("vector", lambda e, pkv=pkv: e.tensor_copy(
                    Sbuf[0], pkv), inc="vc")
            else:
                c_chain[i] = pl.op("vector", lambda e, n=n, i=i, pkv=pkv: e.scalar_tensor_tensor(
                    Sbuf[i % 2], Sbuf[(i - 1) % 2], G128[n], pkv,
                    op0=ALU.mult, op1=ALU.add), inc="vc")
            if i >= 1:
                pl.op("vector", lambda e, i=i, hp=hp: e.tensor_copy(
                    sbf_v[:, i - 1, :], Sbuf[(i - 1) % 2]), inc="vc")
            if c_sin[n] is not None:
                pl.wait("vector", "vc", c_sin[n])
            pl.op("vector", lambda e, n=n, i=i, hp=hp: e.tensor_scalar(
                sinb_v[:, i, :], Sin[:, 64 * n:64 * n + 64], GI[n][i], None,
                op0=ALU.mult), inc="vc")
        c_prep_done = pl.cnt["vc"]

        c_sm = [None] * NCH
        c_ret = [None] * NCH
        c_ret1 = [None] * NCH
        c_sq = [None] * NCH
        c_ommI = [None] * NCH
        c_omm2 = [None] * NCH
        pl.wait("vector", "sq", c_qgc)

        def emit_ret_stt(j):
            po2j = bank[4 + j % 2][:, 0:64]
            pl.wait("vector", "tn", c_omm2[j])
            c_ret[j] = pl.op("vector", lambda e, n=n, j=j, po2j=po2j: e.scalar_tensor_tensor(
                retv[:, j, :], po2j, qgc[:, n:n + 1], rtmp[j % 2],
                op0=ALU.mult, op1=ALU.add,
                accum_out=rsq[:, j:j + 1]), inc="vc")
            pl.wait("scalar", "vc", c_ret[j])
            c_sq[j] = pl.op("scalar", lambda e, j=j: e.activation(
                sqd[j % 2], retv[:, j, :], AF.Square,
                accum_out=rsq[:, NCH + j:NCH + j + 1]), inc="sc")

        for i in range(NCH):
            pst = bank[i % 2]
            po = bank[2 + i % 2][:, 0:64]
            po2 = bank[4 + i % 2][:, 0:64]
            if i >= 2:
                pl.wait("tensor", "vc", c_sm[i - 2])
                pl.wait("tensor", "vc", c_ret1[i - 2])
                pl.wait("tensor", "vc", c_ret[i - 2])
            else:
                pl.wait("tensor", "vc", c_prep_done)
            c_st = pl.op("tensor", lambda e, n=n, i=i, pst=pst: e.matmul(
                pst[:, 0:128], Kp[:, 128 * i:128 * i + 128],
                Qp[:, 128 * i:128 * i + 128], start=True, stop=True), inc="tn")
            pl.wait("vector", "tn", c_st)
            if i >= 2:
                pl.wait("vector", "tn", c_ommI[i - 2])
            pl.wait("vector", "sq", c_mask_n)
            c_sm[i] = pl.op("vector", lambda e, n=n, i=i, pst=pst: e.tensor_tensor(
                smb[i % 2], pst[:, 0:128], masks[n % 2], ALU.mult), inc="vc")
            pl.wait("tensor", "vc", c_sm[i])
            c_ommI[i] = pl.op("tensor", lambda e, i=i, po=po: e.matmul(
                po, smb[i % 2], Vt_v[:, i, :], start=True, stop=True), inc="tn")
            if i >= 1:
                pl.op("tensor", lambda e, i=i, po2=po2: e.matmul(
                    po2, Qp[:, 128 * i:128 * i + 128], sbf_v[:, i - 1, :],
                    start=True, stop=False), inc="tn")
            c_omm2[i] = pl.op("tensor", lambda e, i=i, po2=po2: e.matmul(
                po2, Qp[:, 128 * i:128 * i + 128], sinb_v[:, i, :],
                start=(i == 0), stop=True), inc="tn")
            pl.wait("vector", "tn", c_ommI[i])
            c_ret1[i] = pl.op("vector", lambda e, i=i, po=po: e.tensor_scalar(
                rtmp[i % 2], po, 1.0, None, op0=ALU.mult), inc="vc")
            if i >= 1:
                emit_ret_stt(i - 1)
        emit_ret_stt(NCH - 1)
        cnt_head_tn[n] = pl.cnt["tn"]

        pl.wait("vector", "vc", c_ret[NCH - 1])
        pl.wait("vector", "sc", c_sq[NCH - 1])
        cm = pl.op("vector", lambda e, hp=hp: e.tensor_scalar(
            rsq[:, 2 * NCH:3 * NCH], rsq[:, 0:NCH], 1.0 / 64, None,
            op0=ALU.mult), inc="vc")
        pl.wait("vector", "vc", cm)
        cmm = pl.op("vector", lambda e, hp=hp: e.tensor_tensor(
            rsq[:, 3 * NCH:4 * NCH], rsq[:, 2 * NCH:3 * NCH],
            rsq[:, 2 * NCH:3 * NCH], ALU.mult), inc="vc")
        pl.wait("vector", "vc", cmm)
        cvv = pl.op("vector", lambda e, hp=hp: e.scalar_tensor_tensor(
            rsq[:, 3 * NCH:4 * NCH], rsq[:, NCH:2 * NCH], 1.0 / 64,
            rsq[:, 3 * NCH:4 * NCH], op0=ALU.mult, op1=ALU.subtract), inc="vc")
        pl.wait("scalar", "vc", cvv)
        cst = pl.op("scalar", lambda e, hp=hp: e.activation(
            gnstat[:, 0:NCH], rsq[:, 3 * NCH:4 * NCH], AF.Sqrt,
            bias=epsb), inc="sc")
        cnt_head_sc[n] = cst
        pl.wait("vector", "sc", cst)
        crv = pl.op("vector", lambda e, hp=hp: e.reciprocal(
            gnstat[:, NCH:2 * NCH], gnstat[:, 0:NCH]), inc="vc")
        pl.wait("vector", "vc", crv)
        for i in range(NCH):
            pl.op("vector", lambda e, n=n, i=i, hp=hp: e.tensor_scalar(
                Cv[:, i, 64 * n:64 * n + 64], retv[:, i, :],
                rsq[:, 2 * NCH + i:2 * NCH + i + 1],
                gnstat[:, NCH + i:NCH + i + 1],
                op0=ALU.subtract, op1=ALU.mult), inc="vc")
        cnt_head_vc[n] = pl.cnt["vc"]
        cnt_head_sm[n] = c_sm[NCH - 1]
    c_norm_all = pl.cnt["vc"]
    c_b2_sc = pl.cnt["sc"]

    if "norm" in taps:
        pl.wait("sync", "vc", c_norm_all)
        pl.dma("sync", lambda e: e.dma_start(out=tap_d["norm"][:, :], in_=C), "doo")

    w1v = w1_d.rearrange("(c p) o -> p c o", p=128)
    w2v = w2_d.rearrange("(c p) o -> p c o", p=128)
    c_w1 = pl.dma("gpsimd", lambda e: e.dma_start(
        out=wb12.rearrange("p (c o) -> p c o", o=D), in_=w1v), "gq")
    wbv = wb12.rearrange("p (c o) -> p c o", o=D)

    pl.wait("tensor", "gq", c_w1)
    pl.wait("tensor", "vc", c_norm_all)
    pl.wait("tensor", "sc", c_b2_sc)
    pl.wait("vector", "gq", c_gnbi)
    c_silu = [None] * NCH
    c_u1 = [None] * NCH
    c_t1 = [None] * NCH
    c_attn2 = [None] * NCH
    c_gate_last = None
    for i in range(NCH):
        for dn in range(2):
            pg = bank[(2 * i + dn) % 4]
            if i >= 2:
                pl.wait("tensor", "sc", c_silu[i - 2])
            cg = None
            for u in range(DU):
                cg = pl.op("tensor", lambda e, i=i, dn=dn, u=u, pg=pg: e.matmul(
                    pg, Bv[:, u, 128 * i:128 * i + 128],
                    wbv[:, u, 512 * dn:512 * dn + 512],
                    start=(u == 0), stop=(u == DU - 1)), inc="tn")
            c_gate_last = cg
            pl.wait("scalar", "tn", cg)
            if i >= 2:
                pl.wait("scalar", "vc", c_u1[i - 2])
            c_silu[i] = pl.op("scalar", lambda e, i=i, dn=dn, pg=pg: e.activation(
                sg[i % 2][:, 512 * dn:512 * dn + 512], pg, AF.Silu), inc="sc")
        pl.wait("vector", "sc", c_silu[i])
        pl.wait("vector", "gq", c_gnbi)
        c_u1[i] = pl.op("vector", lambda e, i=i: e.tensor_tensor(
            tmpa, sg[i % 2], gnbi, ALU.add), inc="vc")
        c_t1[i] = pl.op("vector", lambda e, i=i: e.tensor_tensor(
            tmpb, Cv[:, i, :], gnsc, ALU.mult), inc="vc")
        spacer()
        c_attn2[i] = pl.op("vector", lambda e, i=i: e.tensor_tensor(
            Cv[:, i, :], tmpa, tmpb, ALU.add), inc="vc")
    c_attn2_all = pl.cnt["vc"]

    if "attn2" in taps:
        pl.wait("sync", "vc", c_attn2_all)
        pl.dma("sync", lambda e: e.dma_start(out=tap_d["attn2"][:, :], in_=C), "doo")

    c_a2tr = [None] * NCH
    c_a2ev = [None] * NCH
    for i in range(NCH):
        pl.wait("tensor", "vc", c_attn2[i])
        if i >= 2:
            pl.wait("tensor", "vc", c_a2ev[i - 2])
        for u in range(DU):
            c_a2tr[i] = pl.op("tensor", lambda e, i=i, u=u: e.transpose(
                pbt[i % 2][:, 128 * u:128 * u + 128],
                Cv[:, i, 128 * u:128 * u + 128], ident),
                inc="tn" if u == DU - 1 else None)
        pl.wait("vector", "tn", c_a2tr[i])
        c_a2ev[i] = pl.op("vector", lambda e, i=i: e.tensor_copy(
            Bv[:, :, 128 * i:128 * i + 128],
            pbt[i % 2].rearrange("p (u t) -> p u t", t=128)), inc="vc")
    c_c_read_done = c_a2tr[NCH - 1]

    pl.wait("gpsimd", "tn", c_gate_last)
    c_w2 = pl.dma("gpsimd", lambda e: e.dma_start(
        out=wb12.rearrange("p (c o) -> p c o", o=D), in_=w2v), "gq")
    pl.wait("tensor", "gq", c_w2)
    pl.wait("tensor", "sc", c_silu[NCH - 1])
    c_y = [None] * NCH
    c_xf = [None] * NCH
    c_xftr = [None] * NCH
    c_xfev = [None] * NCH
    c_ydma = [None] * NCH
    yv = yspill.ap().rearrange("(i p) d -> p i d", p=128)
    for i in range(NCH):
        pa = [bank[(2 * i) % 4], bank[(2 * i + 1) % 4]]
        pl.wait("tensor", "vc", c_a2ev[i])
        if i >= 2:
            pl.wait("tensor", "vc", c_y[i - 2])
        cw2 = None
        for dn in range(2):
            for u in range(DU):
                cw2 = pl.op("tensor", lambda e, i=i, dn=dn, u=u, pa=pa: e.matmul(
                    pa[dn], Bv[:, u, 128 * i:128 * i + 128],
                    wbv[:, u, 512 * dn:512 * dn + 512],
                    start=(u == 0), stop=(u == DU - 1)), inc="tn")
        pl.wait("vector", "tn", cw2)
        s2 = (i % 2) * 12
        m4 = (i % 4) * 2
        pl.op("vector", lambda e, s2=s2, pa=pa: e.bn_stats(st6[:, s2:s2 + 6], pa[0]))
        cbs = pl.op("vector", lambda e, s2=s2, pa=pa: e.bn_stats(
            st6[:, s2 + 6:s2 + 12], pa[1]), inc="vc")
        spacer()
        ca = pl.op("vector", lambda e, s2=s2, m4=m4: e.bn_aggr(
            mvs[:, m4:m4 + 2], st6[:, s2:s2 + 12]), inc="vc")
        pl.wait("scalar", "vc", ca)
        cs_ = pl.op("scalar", lambda e, i=i, m4=m4: e.activation(
            sdv[:, (i % 4) * 2:(i % 4) * 2 + 1], mvs[:, m4 + 1:m4 + 2],
            AF.Sqrt, bias=epsb), inc="sc")
        pl.wait("vector", "sc", cs_)
        crv = pl.op("vector", lambda e, i=i: e.reciprocal(
            sdv[:, (i % 4) * 2 + 1:(i % 4) * 2 + 2],
            sdv[:, (i % 4) * 2:(i % 4) * 2 + 1]), inc="vc")
        pl.wait("vector", "vc", crv)
        yb = ybuf[i % 2]
        if i >= 1:
            pl.wait("vector", "sq", c_ydma[i - 1])
        for dn in range(2):
            pl.op("vector", lambda e, i=i, dn=dn, m4=m4, pa=pa: e.tensor_scalar(
                ytmp[:, 512 * dn:512 * dn + 512], pa[dn], mvs[:, m4:m4 + 1],
                sdv[:, (i % 4) * 2 + 1:(i % 4) * 2 + 2],
                op0=ALU.subtract, op1=ALU.mult))
        spacer()
        c_y[i] = pl.op("vector", lambda e, i=i, yb=yb: e.tensor_tensor(
            yb, ytmp, Av[:, i, :], ALU.add), inc="vc")
        pl.wait("vector", "vc", c_y[i])
        s2b = (i % 2) * 12
        m4b = (i % 4) * 2 + 8
        pl.op("vector", lambda e, i=i, s2b=s2b, yb=yb: e.bn_stats(
            st6[:, s2b:s2b + 6], yb[:, 0:512]))
        cbs2 = pl.op("vector", lambda e, i=i, s2b=s2b, yb=yb: e.bn_stats(
            st6[:, s2b + 6:s2b + 12], yb[:, 512:1024]), inc="vc")
        spacer()
        ca2 = pl.op("vector", lambda e, s2b=s2b, m4b=m4b: e.bn_aggr(
            mvs[:, m4b:m4b + 2], st6[:, s2b:s2b + 12]), inc="vc")
        pl.wait("scalar", "vc", ca2)
        cs2 = pl.op("scalar", lambda e, i=i, m4b=m4b: e.activation(
            sdv[:, 4 + (i % 2) * 2:4 + (i % 2) * 2 + 1], mvs[:, m4b + 1:m4b + 2],
            AF.Sqrt, bias=epsb), inc="sc")
        pl.wait("vector", "sc", cs2)
        crv2 = pl.op("vector", lambda e, i=i: e.reciprocal(
            sdv[:, 4 + (i % 2) * 2 + 1:4 + (i % 2) * 2 + 2],
            sdv[:, 4 + (i % 2) * 2:4 + (i % 2) * 2 + 1]), inc="vc")
        pl.wait("vector", "vc", crv2)
        if i >= 2 and c_xftr[i - 2] is not None:
            pl.wait("vector", "tn", c_xftr[i - 2])
        c_xf[i] = pl.op("vector", lambda e, i=i, m4b=m4b, yb=yb: e.tensor_scalar(
            tstage[i % 2], yb, mvs[:, m4b:m4b + 1],
            sdv[:, 4 + (i % 2) * 2 + 1:4 + (i % 2) * 2 + 2],
            op0=ALU.subtract, op1=ALU.mult), inc="vc")
        pl.wait("sync", "vc", c_y[i])
        c_ydma[i] = pl.dma("sync", lambda e, i=i, yb=yb: e.dma_start(
            out=yv[:, i, :], in_=yb), "sq")
        pl.wait("tensor", "vc", c_xf[i])
        if i >= 2:
            pl.wait("tensor", "vc", c_xfev[i - 2])
        for u in range(DU):
            c_xftr[i] = pl.op("tensor", lambda e, i=i, u=u: e.transpose(
                pbt[i % 2][:, 128 * u:128 * u + 128],
                tstage[i % 2][:, 128 * u:128 * u + 128], ident),
                inc="tn" if u == DU - 1 else None)
        pl.wait("vector", "tn", c_xftr[i])
        c_xfev[i] = pl.op("vector", lambda e, i=i: e.tensor_copy(
            Bv[:, :, 128 * i:128 * i + 128],
            pbt[i % 2].rearrange("p (u t) -> p u t", t=128)), inc="vc")
    c_xffm = pl.cnt["vc"]
    c_yspill_all = c_ydma[-1]
    c_xftr_last = c_xftr[NCH - 1]

    if "xffm" in taps:
        pl.wait("sync", "vc", c_xffm)
        pl.dma("sync", lambda e: e.dma_start(out=tap_d["xffm"][:, :], in_=B), "doo")

    stack2.close()

    wdb = S3("wdb", [128, NFS * D], BF16)
    wgb = [S3(f"wgb{j}", [128, DU * 512], BF16) for j in range(2)]
    wub = [S3(f"wub{j}", [128, DU * 512], BF16) for j in range(2)]
    gsb = [S3(f"gsb{j}", [128, 512], BF16) for j in range(2)]
    TT = S3("TT", [128, 1024], F32)
    TTb = S3("TTb", [128, 1024], F32)
    LT = S3("LTt", [128, 128], F32)
    LTb = S3("LTb", [128, 128], F32)
    Wsel = S3("Wsel", [128, 128], F32)
    skA = S3("skA", [128, 1024], F32)
    skB = S3("skB", [128, 128], F32)
    csum = S3("csum", [128, 8], F32)
    logit_s = S3("logit_s", [128, 128], F32)

    def stack2_dead(eng):
        pl.wait(eng, "vc", c_xffm)
        pl.wait(eng, "tn", c_xftr_last)
        pl.wait(eng, "sc", pl.cnt["sc"])
        pl.wait(eng, "sq", c_yspill_all)

    gwbv = gwb.rearrange("p (c e) -> p c e", e=NE)
    stack2_dead("vector")
    c_lg_ev = [None] * NCH
    for i in range(NCH):
        plog = bank[4][:, 0:8]
        if i >= 1:
            pl.wait("tensor", "vc", c_lg_ev[i - 1])
        pl.wait("tensor", "vc", c_xfev[i])
        pl.wait("tensor", "gq", c_gw)
        clm_ = None
        for u in range(DU):
            clm_ = pl.op("tensor", lambda e, i=i, u=u, plog=plog: e.matmul(
                plog, Bv[:, u, 128 * i:128 * i + 128], gwbv[:, u, :],
                start=(u == 0), stop=(u == DU - 1)), inc="tn")
        pl.wait("vector", "tn", clm_)
        c_lg_ev[i] = pl.op("vector", lambda e, i=i, plog=plog: e.tensor_copy(
            logit_s[:, 8 * i:8 * i + 8], plog), inc="vc")
    c_logits = pl.cnt["vc"]

    if "logits" in taps:
        pl.wait("sync", "vc", c_logits)
        pl.dma("sync", lambda e: e.dma_start(out=tap_d["logits"][:, :], in_=logit_s), "doo")

    pl.wait("gpsimd", "vc", c_logits)
    stack2_dead("gpsimd")
    c_lb = pl.dma("gpsimd", lambda e: e.dma_start(
        out=cc2_in.ap().rearrange("(i r) e -> r i e", r=128),
        in_=logit_s.rearrange("r (i e) -> r i e", e=NE)), "gq")
    pl.wait("gpsimd", "gq", c_lb)
    c_ag2 = pl.op("gpsimd", lambda e: e.collective_compute(
        "AllGather", ALU.bypass, replica_groups=[list(range(NCORES))],
        ins=[cc2_in.ap().opt()], outs=[cc2_out.ap().opt()]), inc="cc")
    pl.wait("gpsimd", "cc", c_ag2)
    c_g8 = pl.dma("gpsimd", lambda e: e.dma_start(
        out=TTb, in_=cc2_out.ap().rearrange("(p x) e -> p (x e)", p=128)), "gq")

    pl.wait("vector", "gq", c_g8)
    c_dei = pl.op("vector", lambda e: e.tensor_copy(
        TT.rearrange("p (e g) -> p e g", g=128),
        TTb.rearrange("p (g e) -> p e g", e=NE)), inc="vc")

    def gsl(buf, e_):
        return buf[:, 128 * e_:128 * e_ + 128]

    def lsl(buf, e_):
        return buf[:, e_:128:8]

    gcur, gnext = TT, TTb
    lcur, lnext = LT, LTb
    gm = skA[:, 0:128]; gm2 = skA[:, 128:256]
    gt1 = skA[:, 256:384]; gt2 = skA[:, 384:512]
    gt3 = skA[:, 512:640]; gt4 = skA[:, 640:768]
    grs = skA[:, 768:896]; gri = skA[:, 896:1024]
    lm = skB[:, 0:16]; lm2 = skB[:, 16:32]
    lt1 = skB[:, 32:48]; lt2 = skB[:, 48:64]
    lt3 = skB[:, 64:80]; lt4 = skB[:, 80:96]
    lrs = skB[:, 96:112]; lri = skB[:, 112:128]

    def tree2(gc, lc, gout, lout, alu, lsrc=None):
        lc_ = lc if lsrc is None else lsrc
        steps = [
            (gt1, lambda s: (gsl(gc, 0), gsl(gc, 1))),
            (lt1, lambda s: (lsl(lc_, 0), lsl(lc_, 1))),
            (gt2, lambda s: (gsl(gc, 2), gsl(gc, 3))),
            (lt2, lambda s: (lsl(lc_, 2), lsl(lc_, 3))),
            (gt3, lambda s: (gsl(gc, 4), gsl(gc, 5))),
            (lt3, lambda s: (lsl(lc_, 4), lsl(lc_, 5))),
            (gt4, lambda s: (gsl(gc, 6), gsl(gc, 7))),
            (lt4, lambda s: (lsl(lc_, 6), lsl(lc_, 7))),
            (gt1, lambda s: (gt1, gt2)),
            (lt1, lambda s: (lt1, lt2)),
            (gt3, lambda s: (gt3, gt4)),
            (lt3, lambda s: (lt3, lt4)),
            (gout, lambda s: (gt1, gt3)),
        ]
        for dst, srcs in steps:
            a, b = srcs(None)
            pl.op("vector", lambda e, dst=dst, a=a, b=b: e.tensor_tensor(dst, a, b, alu))
        a, b = lt1, lt3
        return pl.op("vector", lambda e, a=a, b=b: e.tensor_tensor(lout, a, b, alu), inc="vc")

    pl.wait("vector", "vc", c_dei)
    cmin = tree2(gcur, lcur, gm, lm, ALU.min, lsrc=logit_s)
    pl.op("vector", lambda e: e.tensor_scalar(gm2, gm, -20.0, None, op0=ALU.mult))
    clm = pl.op("vector", lambda e: e.tensor_scalar(lm2, lm, -20.0, None, op0=ALU.mult), inc="vc")
    pl.wait("vector", "vc", clm)
    cpre = None
    for e_ in range(NE):
        pl.op("vector", lambda e, e_=e_: e.scalar_tensor_tensor(
            gsl(gnext, e_), gsl(gcur, e_), -20.0, gm2, op0=ALU.mult, op1=ALU.subtract))
        cpre = pl.op("vector", lambda e, e_=e_: e.scalar_tensor_tensor(
            lsl(lnext, e_), lsl(logit_s, e_), -20.0, lm2,
            op0=ALU.mult, op1=ALU.subtract), inc="vc")
    pl.wait("scalar", "vc", cpre)
    pl.wait("scalar", "gq", c_g8)
    cexp = None
    for e_ in range(NE):
        pl.op("scalar", lambda e, e_=e_: e.activation(gsl(gcur, e_), gsl(gnext, e_), AF.Exp))
        cexp = pl.op("scalar", lambda e, e_=e_: e.activation(
            lsl(lcur, e_), lsl(lnext, e_), AF.Exp), inc="sc")
    pl.wait("vector", "sc", cexp)

    for it in range(SINK_ITERS):
        crs = tree2(gcur, lcur, grs, lrs, ALU.add)
        pl.wait("vector", "vc", crs)
        pl.op("vector", lambda e: e.reciprocal(gri, grs))
        cri = pl.op("vector", lambda e: e.reciprocal(lri, lrs), inc="vc")
        pl.wait("vector", "vc", cri)
        crm = None
        for e_ in range(NE):
            pl.op("vector", lambda e, e_=e_, gc=gcur, gn=gnext: e.tensor_tensor(
                gsl(gn, e_), gsl(gc, e_), gri, ALU.mult))
            crm = pl.op("vector", lambda e, e_=e_, lc=lcur, ln=lnext: e.tensor_tensor(
                lsl(ln, e_), lsl(lc, e_), lri, ALU.mult), inc="vc")
        gcur, gnext = gnext, gcur
        lcur, lnext = lnext, lcur
        pl.wait("vector", "vc", crm)
        ccs = None
        for e_ in range(NE):
            ccs = pl.op("vector", lambda e, e_=e_, gc=gcur: e.tensor_reduce(
                csum[:, e_:e_ + 1], gsl(gc, e_), AX.X, ALU.add), inc="vc")
        pl.wait("tensor", "vc", ccs)
        pl.wait("tensor", "vc", c_ones)
        pcol = bank[5][0:1, 0:8]
        pbc = bank[5][:, 8:16]
        cmm1 = pl.op("tensor", lambda e, pcol=pcol: e.matmul(
            pcol, onescol, csum, start=True, stop=True), inc="tn")
        pl.wait("vector", "tn", cmm1)
        cdr = pl.op("vector", lambda e, pcol=pcol: e.reciprocal(
            skB[0:1, 0:8], pcol), inc="vc")
        pl.wait("tensor", "vc", cdr)
        cmm2 = pl.op("tensor", lambda e, pbc=pbc: e.matmul(
            pbc, onesrow, skB[0:1, 0:8], start=True, stop=True), inc="tn")
        pl.wait("vector", "tn", cmm2)
        ccm = None
        for e_ in range(NE):
            pl.op("vector", lambda e, e_=e_, pbc=pbc, gc=gcur, gn=gnext: e.tensor_scalar(
                gsl(gn, e_), gsl(gc, e_), pbc[:, e_:e_ + 1], None, op0=ALU.mult))
            ccm = pl.op("vector", lambda e, e_=e_, pbc=pbc, lc=lcur, ln=lnext: e.tensor_scalar(
                lsl(ln, e_), lsl(lc, e_), pbc[:, e_:e_ + 1], None, op0=ALU.mult), inc="vc")
        gcur, gnext = gnext, gcur
        lcur, lnext = lnext, lcur
        pl.wait("vector", "vc", ccm)
        pl.wait("tensor", "vc", ccm)

    if "pi" in taps:
        pl.wait("sync", "vc", pl.cnt["vc"])
        pl.dma("sync", lambda e, gc=gcur: e.dma_start(out=tap_d["pi"][:, :], in_=gc), "doo")
    if "lpi" in taps:
        pl.wait("sync", "vc", pl.cnt["vc"])
        pl.dma("sync", lambda e, lc=lcur: e.dma_start(out=tap_d["lpi"][:, :], in_=lc), "doo")

    def tree1(src, sel, out, alu):
        pl.op("vector", lambda e: e.tensor_tensor(lt1, sel(src, 0), sel(src, 1), alu))
        pl.op("vector", lambda e: e.tensor_tensor(lt2, sel(src, 2), sel(src, 3), alu))
        pl.op("vector", lambda e: e.tensor_tensor(lt3, sel(src, 4), sel(src, 5), alu))
        pl.op("vector", lambda e: e.tensor_tensor(lt4, sel(src, 6), sel(src, 7), alu))
        pl.op("vector", lambda e: e.tensor_tensor(lt1, lt1, lt2, alu))
        spacer()
        pl.op("vector", lambda e: e.tensor_tensor(lt3, lt3, lt4, alu))
        spacer()
        return pl.op("vector", lambda e: e.tensor_tensor(out, lt1, lt3, alu), inc="vc")

    cm1 = tree1(lcur, lsl, lm, ALU.max)
    pl.wait("vector", "vc", cm1)
    sel1 = lnext
    cz = pl.op("vector", lambda e: e.memset(lt1, 0.0), inc="vc")
    pl.wait("vector", "vc", cz)
    for e_ in range(NE):
        c1 = pl.op("vector", lambda e, e_=e_, lc=lcur: e.tensor_tensor(
            lt2, lsl(lc, e_), lm, ALU.is_equal), inc="vc")
        pl.wait("vector", "vc", c1)
        c2 = pl.op("vector", lambda e: e.tensor_tensor(lt3, lt2, lt1, ALU.mult), inc="vc")
        pl.wait("vector", "vc", c2)
        c3 = pl.op("vector", lambda e, e_=e_: e.tensor_tensor(
            lsl(sel1, e_), lt2, lt3, ALU.subtract), inc="vc")
        pl.wait("vector", "vc", c3)
        c4 = pl.op("vector", lambda e, e_=e_: e.tensor_tensor(
            lt1, lt1, lsl(sel1, e_), ALU.add), inc="vc")
        pl.wait("vector", "vc", c4)
    pen = skA[:, 0:128]
    cp = None
    for e_ in range(NE):
        cp = pl.op("vector", lambda e, e_=e_, lc=lcur: e.scalar_tensor_tensor(
            lsl(pen, e_), lsl(sel1, e_), -1e30, lsl(lc, e_),
            op0=ALU.mult, op1=ALU.add), inc="vc")
    pl.wait("vector", "vc", cp)
    cm2 = tree1(pen, lsl, lm2, ALU.max)
    pl.wait("vector", "vc", cm2)
    sel2 = skA[:, 128:256]
    cz = pl.op("vector", lambda e: e.memset(lt1, 0.0), inc="vc")
    pl.wait("vector", "vc", cz)
    for e_ in range(NE):
        c1 = pl.op("vector", lambda e, e_=e_: e.tensor_tensor(
            lt2, lsl(pen, e_), lm2, ALU.is_equal), inc="vc")
        pl.wait("vector", "vc", c1)
        c2 = pl.op("vector", lambda e: e.tensor_tensor(lt3, lt2, lt1, ALU.mult), inc="vc")
        pl.wait("vector", "vc", c2)
        c3 = pl.op("vector", lambda e, e_=e_: e.tensor_tensor(
            lsl(sel2, e_), lt2, lt3, ALU.subtract), inc="vc")
        pl.wait("vector", "vc", c3)
        c4 = pl.op("vector", lambda e, e_=e_: e.tensor_tensor(
            lt1, lt1, lsl(sel2, e_), ALU.add), inc="vc")
        pl.wait("vector", "vc", c4)
    pi0 = lsl(lcur, 0)
    pi1 = lsl(lcur, 1)
    wa = skB[:, 0:16]
    wb_ = skB[:, 16:32]
    c_wsel = None
    for e_ in range(NE):
        pl.op("vector", lambda e, e_=e_: e.tensor_tensor(wa, lsl(sel1, e_), pi0, ALU.mult))
        cwb = pl.op("vector", lambda e, e_=e_: e.tensor_tensor(
            wb_, lsl(sel2, e_), pi1, ALU.mult), inc="vc")
        pl.wait("vector", "vc", cwb)
        c_wsel = pl.op("vector", lambda e, e_=e_: e.tensor_tensor(
            lsl(Wsel, e_), wa, wb_, ALU.add), inc="vc")

    if "wsel" in taps:
        pl.wait("sync", "vc", c_wsel)
        pl.dma("sync", lambda e: e.dma_start(out=tap_d["wsel"][:, :], in_=Wsel), "doo")

    wdbv = wdb.rearrange("p (j d) -> p j d", d=D)
    wgbv = [wgb[j].rearrange("p (c f) -> p c f", f=512) for j in range(2)]
    wubv = [wub[j].rearrange("p (c f) -> p c f", f=512) for j in range(2)]
    Hs = C.rearrange("p (j t) -> p j t", t=NTOK)

    c_wg_dma = {}
    c_wu_dma = {}
    c_wd_dma = {}
    c_gu_ev = {}
    c_silu_ev = {}
    c_down_stt = {}
    c_down_mm = {}
    c_gu_mm = {}

    blkseq = [(ex, s, h) for ex in range(NE) for s in range(NSLAB) for h in range(2)]

    first = True
    for bi, (ex, s, half) in enumerate(blkseq):
        buf = bi % 2
        if bi < 2:
            stack2_dead("gpsimd")
        if bi >= 2:
            pl.wait("gpsimd", "tn", c_gu_mm[blkseq[bi - 2]])
        fc0 = BSLAB * s + 512 * half
        srcg = wg_d[ex].rearrange("(c p) f -> p c f", p=128)[:, :, fc0:fc0 + 512]
        c_wg_dma[(ex, s, half)] = pl.dma("gpsimd", lambda e, srcg=srcg, buf=buf: e.dma_start(
            out=wgbv[buf], in_=srcg), "gq")
        srcu = wu_d[ex].rearrange("(c p) f -> p c f", p=128)[:, :, fc0:fc0 + 512]
        c_wu_dma[(ex, s, half)] = pl.dma("gpsimd", lambda e, srcu=srcu, buf=buf: e.dma_start(
            out=wubv[buf], in_=srcu), "gq")
        if half == 0:
            if bi >= 2:
                pl.wait("gpsimd", "tn", c_down_mm[blkseq[bi - 1][0:2]])
            srcd = wd_d[ex].rearrange("(s j p) d -> s p j d", s=NSLAB, p=128)[s]
            c_wd_dma[(ex, s)] = pl.dma("gpsimd", lambda e, srcd=srcd: e.dma_start(
                out=wdbv, in_=srcd), "gq")

        pl.wait("tensor", "gq", c_wg_dma[(ex, s, half)])
        pl.wait("tensor", "gq", c_wu_dma[(ex, s, half)])
        if first:
            pl.wait("tensor", "tn", c_c_read_done)
            pl.wait("tensor", "vc", c_xffm)
            first = False
        for j4 in range(4):
            jj = 4 * half + j4
            for tq in range(4):
                k = j4 * 4 + tq
                pgb = bank[k % 2]
                pub = bank[2 + k % 2]
                if k >= 2:
                    pj, ptq = divmod(k - 2, 4)
                    pl.wait("tensor", "sc", c_silu_ev[(ex, s, half, pj, ptq)])
                    pl.wait("tensor", "vc", c_gu_ev[(ex, s, 4 * half + pj, ptq)])
                elif bi >= 1:
                    ex2, s2, h2 = blkseq[bi - 1]
                    pl.wait("tensor", "sc", c_silu_ev[(ex2, s2, h2, 3, 3)])
                    pl.wait("tensor", "vc", c_gu_ev[(ex2, s2, 4 * h2 + 3, 3)])
                cgmm = None
                for u in range(DU):
                    cgmm = pl.op("tensor", lambda e, u=u, j4=j4, tq=tq, buf=buf, pgb=pgb: e.matmul(
                        pgb, wgbv[buf][:, u, 128 * j4:128 * j4 + 128],
                        Bv[:, u, 512 * tq:512 * tq + 512],
                        start=(u == 0), stop=(u == DU - 1)), inc="tn")
                cumm = None
                for u in range(DU):
                    cumm = pl.op("tensor", lambda e, u=u, j4=j4, tq=tq, buf=buf, pub=pub: e.matmul(
                        pub, wubv[buf][:, u, 128 * j4:128 * j4 + 128],
                        Bv[:, u, 512 * tq:512 * tq + 512],
                        start=(u == 0), stop=(u == DU - 1)), inc="tn")
                c_gu_mm[(ex, s, half)] = cumm
                pl.wait("scalar", "tn", cgmm)
                kk2 = (ex, s, half, j4, tq)
                if k >= 2:
                    pj, ptq = divmod(k - 2, 4)
                    pl.wait("scalar", "vc", c_gu_ev[(ex, s, 4 * half + pj, ptq)])
                elif bi >= 1:
                    ex2, s2, h2 = blkseq[bi - 1]
                    pl.wait("scalar", "vc", c_gu_ev[(ex2, s2, 4 * h2 + 3, 3)])
                c_silu_ev[kk2] = pl.op("scalar", lambda e, k=k, pgb=pgb: e.activation(
                    gsb[k % 2], pgb, AF.Silu), inc="sc")
                pl.wait("vector", "tn", cumm)
                pl.wait("vector", "sc", c_silu_ev[kk2])
                c_gu_ev[(ex, s, jj, tq)] = pl.op(
                    "vector", lambda e, jj=jj, tq=tq, k=k, pub=pub: e.tensor_tensor(
                        Hs[:, jj, 512 * tq:512 * tq + 512], gsb[k % 2], pub,
                        ALU.mult), inc="vc")

        if half == 1:
            pl.wait("tensor", "gq", c_wd_dma[(ex, s)])
            for jj in range(NFS):
                for tq in range(4):
                    pl.wait("tensor", "vc", c_gu_ev[(ex, s, jj, tq)])
            for tc in range(NCH):
                for dn in range(2):
                    k = tc * 2 + dn
                    pdb = bank[4 + k % 2]
                    if k >= 2:
                        pl.wait("tensor", "vc",
                                c_down_stt[(ex, s, (k - 2) // 2, (k - 2) % 2)])
                    cdmm = None
                    for jj in range(NFS):
                        cdmm = pl.op("tensor", lambda e, jj=jj, tc=tc, dn=dn, pdb=pdb: e.matmul(
                            pdb, Hs[:, jj, 128 * tc:128 * tc + 128],
                            wdbv[:, jj, 512 * dn:512 * dn + 512],
                            start=(jj == 0), stop=(jj == NFS - 1)), inc="tn")
                    c_down_mm[(ex, s)] = cdmm
                    pl.wait("vector", "tn", cdmm)
                    pl.wait("vector", "vc", c_wsel)
                    wcol = Wsel[:, 8 * tc + ex:8 * tc + ex + 1]
                    dst = Av[:, tc, 512 * dn:512 * dn + 512]
                    if ex == 0 and s == 0:
                        c_down_stt[(ex, s, tc, dn)] = pl.op(
                            "vector", lambda e, pdb=pdb, wcol=wcol, dst=dst: e.tensor_scalar(
                                dst, pdb, wcol, None, op0=ALU.mult), inc="vc")
                    else:
                        c_down_stt[(ex, s, tc, dn)] = pl.op(
                            "vector", lambda e, pdb=pdb, wcol=wcol, dst=dst: e.scalar_tensor_tensor(
                                dst, pdb, wcol, dst, op0=ALU.mult, op1=ALU.add), inc="vc")
    c_ffn_done = pl.cnt["vc"]

    if "fpre" in taps:
        pl.wait("sync", "vc", c_ffn_done)
        pl.dma("sync", lambda e: e.dma_start(out=tap_d["fpre"][:, :], in_=A), "doo")

    stack3.close()

    yin = [S4(f"yin{j}", [128, 2 * D], F32) for j in range(2)]
    outst = [S4(f"outst{j}", [128, 2 * D], F32) for j in range(2)]
    ftmp = [S4(f"ftmp{j}", [128, D], F32) for j in range(2)]
    st6f = S4("st6f", [128, 24], F32)
    mvf = S4("mvf", [128, 8], F32)
    sdf = S4("sdf", [128, 8], F32)

    c_yin = [None] * (NCH // 2)
    c_odma = [None] * (NCH // 2)
    c_out_done = [None] * (NCH // 2)
    outv = out_d.rearrange("(i p) d -> p i d", p=128)
    pl.wait("vector", "vc", c_ffn_done)
    pl.wait("sync", "vc", c_ffn_done)
    pl.wait("sync", "sc", pl.cnt["sc"])
    pl.wait("sync", "tn", pl.cnt["tn"])
    pl.wait("sync", "sq", c_yspill_all)
    for i in range(NCH):
        g = i // 2
        if i % 2 == 0:
            if g >= 2:
                pl.wait("sync", "vc", c_out_done[g - 2])
                pl.wait("sync", "doo", c_odma[g - 2])
            c_yin[g] = pl.dma("sync", lambda e, g=g: e.dma_start(
                out=yin[g % 2].rearrange("p (j d) -> p j d", d=D),
                in_=yv[:, 2 * g:2 * g + 2, :]), "sq")
        s2 = (i % 2) * 12
        m4 = (i % 4) * 2
        pl.op("vector", lambda e, i=i, s2=s2: e.bn_stats(
            st6f[:, s2:s2 + 6], Av[:, i, 0:512]))
        cbs = pl.op("vector", lambda e, i=i, s2=s2: e.bn_stats(
            st6f[:, s2 + 6:s2 + 12], Av[:, i, 512:1024]), inc="vc")
        spacer()
        ca = pl.op("vector", lambda e, s2=s2, m4=m4: e.bn_aggr(
            mvf[:, m4:m4 + 2], st6f[:, s2:s2 + 12]), inc="vc")
        pl.wait("scalar", "vc", ca)
        cs_ = pl.op("scalar", lambda e, i=i, m4=m4: e.activation(
            sdf[:, (i % 4) * 2:(i % 4) * 2 + 1], mvf[:, m4 + 1:m4 + 2],
            AF.Sqrt, bias=epsb), inc="sc")
        pl.wait("vector", "sc", cs_)
        crv = pl.op("vector", lambda e, i=i: e.reciprocal(
            sdf[:, (i % 4) * 2 + 1:(i % 4) * 2 + 2],
            sdf[:, (i % 4) * 2:(i % 4) * 2 + 1]), inc="vc")
        pl.wait("vector", "vc", crv)
        cf = pl.op("vector", lambda e, i=i, m4=m4: e.tensor_scalar(
            ftmp[i % 2], Av[:, i, :], mvf[:, m4:m4 + 1],
            sdf[:, (i % 4) * 2 + 1:(i % 4) * 2 + 2],
            op0=ALU.subtract, op1=ALU.mult), inc="vc")
        pl.wait("vector", "sq", c_yin[g])
        pl.wait("vector", "vc", cf)
        if g >= 2 and i % 2 == 0:
            pl.wait("vector", "doo", c_odma[g - 2])
        c_out_done[g] = pl.op("vector", lambda e, i=i, g=g: e.tensor_tensor(
            outst[g % 2][:, (i % 2) * D:(i % 2) * D + D], ftmp[i % 2],
            yin[g % 2][:, (i % 2) * D:(i % 2) * D + D], ALU.add), inc="vc")
        if i % 2 == 1:
            pl.wait("sync", "vc", c_out_done[g])
            c_odma[g] = pl.dma("sync", lambda e, g=g: e.dma_start(
                out=outv[:, 2 * g:2 * g + 2, :],
                in_=outst[g % 2].rearrange("p (j d) -> p j d", d=D)), "doo")

    pl.wait("sync", "doo", pl.cnt["doo"])
    pl.wait("gpsimd", "doo", pl.cnt["doo"])

    with nc.Block() as block:
        def runner(eng_name):
            def _run(h):
                for item in pl.ops[eng_name]:
                    if item[0] == "w":
                        h.wait_ge(sems[item[1]], item[2])
                    else:
                        _, fn, inc, nn = item
                        r = fn(h)
                        if inc is not None:
                            r.then_inc(sems[inc], nn)
            return _run
        block.sync(runner("sync"))
        block.gpsimd(runner("gpsimd"))
        block.tensor(runner("tensor"))
        block.vector(runner("vector"))
        block.scalar(runner("scalar"))

    stack4.close()
    stack1.close()
    return nc, tap_d


_BUILD_CACHE = {}


def _host_inputs(inputs):
    g64 = _gammas64()
    x = np.ascontiguousarray(np.asarray(inputs["L0_attn_x"], dtype=np.float32))
    wq = np.asarray(inputs["L0_attn_wq"], dtype=np.float32)
    wk = np.asarray(inputs["L0_attn_wk"], dtype=np.float32)
    wv = np.asarray(inputs["L0_attn_wv"], dtype=np.float32)

    def dup(w):
        a = np.ascontiguousarray(w.transpose(1, 0, 2).reshape(HD, NH * HD))
        return np.concatenate([a, a], axis=0)
    wqkv = np.ascontiguousarray(np.concatenate([dup(wq), dup(wk), dup(wv)], axis=1))

    gns = np.asarray(inputs["L0_attn_gn_scale"], dtype=np.float32)
    gnb = np.asarray(inputs["L0_attn_gn_bias"], dtype=np.float32)
    gnscb = np.ascontiguousarray(np.broadcast_to(gns[None, :], (128, D)))
    gnbib = np.ascontiguousarray(np.broadcast_to(gnb[None, :], (128, D)))

    gwT = np.ascontiguousarray(np.asarray(inputs["L0_ffn_gate_W"], np.float32).T)
    WgT = np.ascontiguousarray(np.asarray(inputs["L0_ffn_W_gate"], np.float32).transpose(0, 2, 1))
    WuT = np.ascontiguousarray(np.asarray(inputs["L0_ffn_W_up"], np.float32).transpose(0, 2, 1))
    WdT = np.ascontiguousarray(np.asarray(inputs["L0_ffn_W_down"], np.float32).transpose(0, 2, 1))

    ident = np.eye(128, dtype=ml_dtypes.bfloat16)

    r = np.arange(CH, dtype=np.float64)
    maskT = np.zeros((NH, CH, CH), np.float64)
    for n in range(NH):
        expo = r[None, :] - r[:, None]
        maskT[n] = np.where(expo >= 0, g64[n] ** np.maximum(expo, 0), 0.0)
    maskT = maskT.astype(np.float32)

    qgc = np.zeros((CH, NH), np.float64)
    for n in range(NH):
        qgc[:, n] = g64[n] ** (r + 1)
    qgc = qgc.astype(np.float32)

    kg = np.zeros((CH, NH), np.float64)
    for n in range(NH):
        kg[:, n] = g64[n] ** (CH - 1 - r)
    kg = kg.astype(np.float32)

    wsbs = []
    for c in range(NCORES):
        b, cc_ = divmod(c, 4)
        w = np.zeros((HD, NH * NCORES), np.float64)
        for n in range(NH):
            for c2 in range(NCORES):
                b2, cp = divmod(c2, 4)
                if b2 == b and cp < cc_:
                    w[:, 8 * n + c2] = g64[n] ** (2048.0 * (cc_ - cp - 1))
        wsbs.append(w.astype(np.float32))

    common = {
        "wqkv": wqkv,
        "w1": np.ascontiguousarray(np.asarray(inputs["L0_attn_w1"], np.float32)),
        "w2": np.ascontiguousarray(np.asarray(inputs["L0_attn_w2"], np.float32)),
        "gnscb": gnscb, "gnbib": gnbib, "gwT": gwT,
        "WgT": WgT, "WuT": WuT, "WdT": WdT, "ident": ident,
        "maskT": maskT, "qgc": qgc, "kg": kg,
    }
    in_maps = []
    for c in range(NCORES):
        b, cc_ = divmod(c, 4)
        m = dict(common)
        m["x0"] = np.ascontiguousarray(x[b, 2048 * cc_:2048 * (cc_ + 1), :])
        m["wsb"] = wsbs[c]
        in_maps.append(m)
    return in_maps


def kernel(L0_attn_x, L0_attn_wq, L0_attn_wk, L0_attn_wv, L0_attn_w1,
           L0_attn_w2, L0_attn_gn_scale, L0_attn_gn_bias, L0_ffn_gate_W,
           L0_ffn_W_gate, L0_ffn_W_up, L0_ffn_W_down,
           _taps=(), _return_res=False):
    inputs = dict(
        L0_attn_x=L0_attn_x, L0_attn_wq=L0_attn_wq, L0_attn_wk=L0_attn_wk,
        L0_attn_wv=L0_attn_wv, L0_attn_w1=L0_attn_w1, L0_attn_w2=L0_attn_w2,
        L0_attn_gn_scale=L0_attn_gn_scale, L0_attn_gn_bias=L0_attn_gn_bias,
        L0_ffn_gate_W=L0_ffn_gate_W, L0_ffn_W_gate=L0_ffn_W_gate,
        L0_ffn_W_up=L0_ffn_W_up, L0_ffn_W_down=L0_ffn_W_down)
    key = tuple(sorted(_taps))
    if key not in _BUILD_CACHE:
        _BUILD_CACHE[key] = build(taps=_taps)
    nc, _ = _BUILD_CACHE[key]
    in_maps = _host_inputs(inputs)
    res = run_bass_kernel_spmd(nc, in_maps, core_ids=list(range(NCORES)))
    out = np.zeros((2, 8192, D), np.float32)
    for c in range(NCORES):
        b, cc_ = divmod(c, 4)
        out[b, 2048 * cc_:2048 * (cc_ + 1), :] = res.results[c]["out"]
    if _return_res:
        return out, res
    return out
